# version 10
# baseline (speedup 1.0000x reference)
import numpy as np
import ml_dtypes
import concourse.bass as bass
import concourse.mybir as mybir
from concourse.bass_utils import run_bass_kernel_spmd

F32 = mybir.dt.float32
BF16 = mybir.dt.bfloat16
AF = mybir.ActivationFunctionType
ALU = mybir.AluOpType
AX = mybir.AxisListType
bfdt = ml_dtypes.bfloat16

S = 2048
DM = 1024
NH = 16
DK = 64
HFF = 2048
NE = 8
NM = DM // 128
NTT = S // 128

NBYTES = 212352
NB2 = NBYTES // 2
NB4 = NBYTES // 4

O_XNT   = 0
O_XST2  = 0
O_LNW   = 8192
O_WOT   = 32768
O_W1B   = 32768
O_W3B   = 36864
O_S1    = 40960
O_W2_A  = 45056
O_WQKVT = 49152
O_ESB   = 49152
O_X2    = 49152
O_W2_B  = 49152
O_AO    = 65536
O_HH_HI = 106496
O_CS_C  = 98304
O_CS_S  = 102400
O_INVRZ = 98304
O_INVB  = 100352
O_TMP   = 102400
O_M4    = 94208
O_ESBB  = 69632


def ao_off(j):
    return 205312 if j == 0 else O_Q + 4096 * (j - 1)
O_W2_C  = 98304
O_Q     = 106496
O_K     = 139264
O_STAGE = 106496
def oa_off(tt):
    return 65536 + 4096 * tt if tt < 8 else 139264 + 4096 * (tt - 8)
O_VAUG  = 172032
O_XNST  = 172032
O_QS    = 49152
def O_RSAs(sl):
    return 53248 + 4096 * sl
def O_RSBs(sl):
    return 65536 + 4096 * sl
O_RS1   = 205312
O_W2_D  = 205312
O_HH_LO = O_VAUG
O_CPSW  = 209408
O_CPSH  = 209664
O_CMLOG = 209920
O_CID   = 210944
O_CONES = 211200
O_WGS   = 211456
O_STATS = 211584
O_GS    = 211840
O_XST   = 65536
O_LS    = 77824
O_GT    = 78336


def w2_off(ht):
    if ht < 8:
        return O_W2_B + 2048 * ht
    if ht < 12:
        return O_W2_C + 2048 * (ht - 8)
    if ht < 14:
        return O_W2_A + 2048 * (ht - 12)
    return O_W2_D + 2048 * (ht - 14)


def hh_off(ht):
    return (O_HH_LO + 4096 * ht) if ht < 8 else (O_HH_HI + 4096 * (ht - 8))


class Sched:
    def __init__(self, sems):
        self.sems = sems
        self.cnt = {k: 0 for k in sems}
        self.ops = []
        self.hiwater = {}

    def ev(self, sem):
        return (sem, self.cnt[sem])

    def op(self, eng, emit, waits=(), inc=None):
        ws = []
        for w in waits:
            if w is None:
                continue
            sname, val = w
            if val <= 0:
                continue
            key = (eng, sname)
            if self.hiwater.get(key, 0) >= val:
                continue
            self.hiwater[key] = val
            ws.append((sname, val))
        if inc is not None:
            self.cnt[inc[0]] += inc[1]
        self.ops.append((eng, ws, emit, inc))
        return (inc[0], self.cnt[inc[0]]) if inc is not None else None

    def emit(self, eng_name, engine):
        for (e, ws, fn, inc) in self.ops:
            if e != eng_name:
                continue
            for (sname, val) in ws:
                engine.wait_ge(self.sems[sname], val)
            inst = fn(engine)
            if inc is not None:
                inst.then_inc(self.sems[inc[0]], inc[1])


def build(n_phases=99, taps=(), nblocks=32):
    nc = bass.Bass("TRN2", target_bir_lowering=False)
    x_in  = nc.dram_tensor("x",    [S, DM],        F32, kind="ExternalInput")
    wqkv  = nc.dram_tensor("wqkv", [3 * DM, DM],   F32, kind="ExternalInput")
    wo    = nc.dram_tensor("wo",   [DM, DM],       F32, kind="ExternalInput")
    wg    = nc.dram_tensor("wg",   [DM, NE],       F32, kind="ExternalInput")
    w1    = nc.dram_tensor("w1",   [NE, DM, HFF],  F32, kind="ExternalInput")
    w3    = nc.dram_tensor("w3",   [NE, DM, HFF],  F32, kind="ExternalInput")
    w2    = nc.dram_tensor("w2",   [NE, HFF, DM],  F32, kind="ExternalInput")
    ctab  = nc.dram_tensor("ctab", [128, S],   BF16, kind="ExternalInput")
    stab  = nc.dram_tensor("stab", [128, S],   BF16, kind="ExternalInput")
    psw   = nc.dram_tensor("psw",  [128, 128], BF16, kind="ExternalInput")
    psh   = nc.dram_tensor("psh",  [128, 128], BF16, kind="ExternalInput")
    mlog  = nc.dram_tensor("mlog", [128, 2048], BF16, kind="ExternalInput")
    ident = nc.dram_tensor("ident", [128, 128], BF16, kind="ExternalInput")
    out   = nc.dram_tensor("out",  [S, DM],        F32, kind="ExternalOutput")
    tap_dram = {}
    for (name, rows, cols, dt, off) in taps:
        tap_dram[name] = nc.dram_tensor("tap_" + name, [rows, cols],
                                        dt, kind="ExternalOutput")

    import contextlib
    with contextlib.ExitStack() as stack:
        ec = stack.enter_context
        AR = ec(nc.sbuf_tensor("AR", [128, NB2], BF16))
        PA = ec(nc.psum_tensor("PA", [128, 2048], F32))
        PB = ec(nc.psum_tensor("PB", [128, 2048], F32))
        _sG = ec(nc.semaphore("sG"))
        _sS = ec(nc.semaphore("sS"))
        _sT = ec(nc.semaphore("sT"))
        _sV = ec(nc.semaphore("sV"))
        _sC = ec(nc.semaphore("sC"))
        block = ec(nc.Block())
        ARf = AR.bitcast(F32)
        sems = {"g": _sG, "s": _sS, "t": _sT, "v": _sV, "c": _sC}
        sc = Sched(sems)

        def bslc(byte_off, p0, pn, cols):
            return AR[p0:p0 + pn, byte_off // 2: byte_off // 2 + cols]

        def bap(byte_off, dims):
            return bass.AP(AR, byte_off // 2, [[NB2, 128]] + dims)

        def fslc(byte_off, p0, pn, cols):
            return ARf[p0:p0 + pn, byte_off // 4: byte_off // 4 + cols]

        def fap(byte_off, dims):
            return bass.AP(ARf, byte_off // 4, [[NB4, 128]] + dims)

        def bap3(byte_off, step, n, inner):
            return bass.AP(AR, byte_off // 2, [[NB2, 128], [step, n], [1, inner]])

        def ST(col, n=1, p0=0, pn=128):
            return fslc(O_STATS + 4 * col, p0, pn, n)

        ap_eps = fslc(O_STATS + 4 * 18, 0, 128, 1)
        ap_zero = fslc(O_STATS + 4 * 19, 0, 128, 1)
        nc.const_aps.register(F32, 1e-5, ap_eps)
        nc.const_aps.register(F32, 0.0, ap_zero)

        CPSW = bslc(O_CPSW, 0, 128, 128)

        sc.op("g", lambda g: g.dma_start(CPSW, psw.ap()), inc=("g", 16))
        sc.op("g", lambda g: g.dma_start(bslc(O_CS_C, 0, 128, S), ctab.ap()), inc=("g", 16))
        sc.op("g", lambda g: g.dma_start(bslc(O_CS_S, 0, 128, S), stab.ap()), inc=("g", 16))
        for m in range(NM):
            sc.op("g", lambda g, m=m: g.dma_start(
                bslc(O_WGS + 2 * 8 * m, 0, 128, 8),
                wg[128 * m:128 * m + 128, :]), inc=("g", 16))
        for j in range(0, 24, 2):
            sc.op("g", lambda g, j=j: g.dma_start(
                bap3(O_STAGE + 2048 * j, 1024, 2, 1024),
                bass.AP(wqkv, 128 * 1024 * j,
                        [[1024, 128], [128 * 1024, 2], [1, 1024]])),
                inc=("g", 16))
        for j in range(0, 8, 2):
            sc.op("g", lambda g, j=j: g.dma_start(
                bap3(O_STAGE + 2048 * (24 + j), 1024, 2, 1024),
                bass.AP(wo, 128 * 1024 * j,
                        [[1024, 128], [128 * 1024, 2], [1, 1024]])),
                inc=("g", 16))
        ev_gload = sc.ev("g")
        sc.op("v", lambda v: v.memset(bslc(O_CONES, 0, 128, 128), 1.0))
        sc.op("v", lambda v: v.memset(fslc(O_STATS + 4 * 18, 0, 128, 1), 1e-5))
        ev_consts = sc.op("v", lambda v: v.memset(
            fslc(O_STATS + 4 * 19, 0, 128, 1), 0.0), inc=("v", 1))

        LNW = O_LNW

        def LW(col, n=1):
            return fslc(LNW + 4 * col, 0, 128, n)

        ev_rs = [None] * NTT
        ev_sq = [None] * NTT
        ev_xload = [None] * NTT
        for tt in range(NTT):
            par = tt % 2
            xs = fslc(O_XST2 + 4 * DM * par, 0, 128, DM)
            w = []
            if tt >= 2:
                w = [ev_rs[tt - 2], ev_sq[tt - 2]]
            ev_xload[tt] = sc.op("s", lambda s, tt=tt, xs=xs: s.dma_start(
                xs, x_in[128 * tt:128 * tt + 128, :]), waits=w, inc=("s", 16))
            ev_rs[tt] = sc.op("v", lambda v, xs=xs, tt=tt: v.reduce_sum(
                LW(tt), xs, axis=AX.X), waits=[ev_xload[tt]], inc=("v", 1))
            ev_sq[tt] = sc.op("c", lambda c, xs=xs, tt=tt, par=par: c.activation(
                bslc(O_RS1 + 2048 * par, 0, 128, 1024), xs, AF.Square,
                accum_out=LW(16 + tt)),
                waits=[ev_xload[tt], ev_consts], inc=("c", 1))
        ev_rs_all = sc.ev("v")
        ev_sq_all = sc.ev("c")

        def vdummy():
            sc.op("v", lambda v: v.memset(fslc(O_STATS + 4 * 62, 0, 128, 1), 0.0))

        def cdummy():
            sc.op("c", lambda c: c.activation(
                fslc(O_STATS + 4 * 61, 0, 128, 1),
                fslc(O_STATS + 4 * 19, 0, 128, 1), AF.Identity))

        sc.op("v", lambda v: v.tensor_scalar(
            LW(32, 16), LW(0, 16), 1.0 / DM, None, ALU.mult), waits=[ev_sq_all])
        vdummy()
        sc.op("v", lambda v: v.tensor_tensor(
            LW(48, 16), LW(32, 16), LW(32, 16), ALU.mult))
        vdummy()
        ev_var = sc.op("v", lambda v: v.scalar_tensor_tensor(
            LW(64, 16), LW(16, 16), 1.0 / DM, LW(48, 16), ALU.mult, ALU.subtract),
            inc=("v", 1))
        ev_ln = sc.op("c", lambda c: c.activation(
            LW(80, 16), LW(64, 16), AF.Ln, bias=1e-5),
            waits=[ev_var], inc=("c", 1))
        ev_lh = sc.op("v", lambda v: v.tensor_scalar(
            LW(96, 16), LW(80, 16), -0.5, None, ALU.mult),
            waits=[ev_ln], inc=("v", 1))
        ev_rstd = sc.op("c", lambda c: c.activation(
            LW(112, 16), LW(96, 16), AF.Exp), waits=[ev_lh], inc=("c", 1))
        cdummy()
        ev_ngm = sc.op("v", lambda v: v.scalar_tensor_tensor(
            LW(128, 16), LW(32, 16), -1.0, LW(112, 16), ALU.mult, ALU.mult),
            waits=[ev_rstd], inc=("v", 1))
        ev_xn = [None] * NTT
        for tt in range(NTT):
            par = tt % 2
            xs = fslc(O_XST2 + 4 * DM * par, 0, 128, DM)
            w = [ev_rs_all, ev_sq_all] if tt < 2 else [ev_xn[tt - 2]]
            ev_x2l = sc.op("s", lambda s, tt=tt, xs=xs: s.dma_start(
                xs, x_in[128 * tt:128 * tt + 128, :]), waits=w, inc=("s", 16))
            ev_xn[tt] = sc.op("c", lambda c, xs=xs, tt=tt: c.activation(
                bslc(O_XNST + 2048 * tt, 0, 128, DM), xs, AF.Identity,
                bias=LW(128 + tt), scale=LW(112 + tt)),
                waits=[ev_x2l, ev_ngm], inc=("c", 1))
        ev_ln_all = sc.ev("c")
        ev_xload_all = sc.ev("s")

        sc.op("s", lambda s: s.dma_start(
            bass.AP(AR, O_WQKVT // 2, [[NB2, 128], [128, 192], [1, 128]]),
            bslc(O_STAGE, 0, 128, 24 * DM), transpose=True),
            waits=[ev_gload, ev_xload_all], inc=("s", 16))
        sc.op("s", lambda s: s.dma_start(
            bass.AP(AR, O_WOT // 2, [[NB2, 128], [128, 64], [1, 128]]),
            bslc(O_STAGE + 2048 * 24, 0, 128, 8 * DM), transpose=True),
            inc=("s", 16))
        sc.op("s", lambda s: s.dma_start(
            bass.AP(AR, O_XNT // 2, [[NB2, 128], [128, 128], [1, 128]]),
            bslc(O_XNST, 0, 128, 16 * DM), transpose=True),
            inc=("s", 16))
        EV_TPOSE = sc.ev("s")

        if n_phases <= 0:
            return _finish(nc, sc, block, taps, tap_dram, AR, ARf, [EV_TPOSE])

        ev_qkv = [None] * 16
        ev_qevac = [None] * 16
        ev_qsw = [None] * 16
        ev_qsevac = [None] * 16
        ev_rope = [None] * 16
        ev_t2 = [None] * 16
        dst_prev = None
        for j in range(16):
            P = PA if j % 2 == 0 else PB
            dst_off = (O_Q + 4096 * j) if j < 8 else (O_K + 4096 * (j - 8))
            w0 = [EV_TPOSE] + ([ev_qevac[j - 2]] if j >= 2 else [])
            ev = None
            for m in range(NM):
                for c in range(4):
                    ev = sc.op("t", lambda t, P=P, m=m, c=c, j=j: t.matmul(
                        P[:, 512 * c:512 * c + 512],
                        bslc(O_WQKVT + 2 * 128 * (8 * j + m), 0, 128, 128),
                        bap3(O_XNT + 2 * 128 * (32 * c + m), 1024, 4, 128),
                        start=(m == 0), stop=(m == NM - 1)),
                        waits=w0 if (m == 0 and c == 0) else [],
                        inc=("t", 1) if (m == NM - 1 and c == 3) else None)
            ev_qkv[j] = ev
            scl = 0.125 if j < 8 else 1.0
            ev_qevac[j] = sc.op("c", lambda c_, P=P, dst_off=dst_off, scl=scl:
                c_.activation(bslc(dst_off, 0, 128, 2048), P[:, :], AF.Identity,
                              scale=scl),
                waits=[ev_qkv[j]], inc=("c", 1))
        ev_qsev_ch = {}

        def emit_qswap(j):
            dst_off = (O_Q + 4096 * j) if j < 8 else (O_K + 4096 * (j - 8))
            for c in range(4):
                gidx = 4 * j + c
                pb = gidx % 2
                Psw = PB[:, 1024 + 512 * pb:1536 + 512 * pb]
                w = [ev_qevac[j]]
                if gidx >= 2:
                    w.append(ev_qsev_ch[gidx - 2])
                if j == 0 and c < 2:
                    w.append(ev_qevac[15])
                ev_m = sc.op("t", lambda t, Psw=Psw, c=c, dst_off=dst_off: t.matmul(
                    Psw, CPSW, bslc(dst_off + 1024 * c, 0, 128, 512),
                    start=True, stop=True),
                    waits=w, inc=("t", 1))
                ev_qsev_ch[gidx] = sc.op("c", lambda c_, Psw=Psw, c=c: c_.activation(
                    bslc(O_QS + 1024 * c, 0, 128, 512), Psw, AF.Identity),
                    waits=[ev_m] + ([ev_t2[j - 1]] if (j >= 1 and c == 0) else []),
                    inc=("c", 1))
            sl = j % 3
            sc.op("v", lambda v, dst_off=dst_off, sl=sl: v.tensor_tensor(
                bslc(O_RSAs(sl), 0, 128, 2048), bslc(dst_off, 0, 128, 2048),
                bslc(O_CS_C, 0, 128, 2048), ALU.mult),
                waits=[ev_qevac[j]] + ([ev_qevac[15]] if j == 0 else []))
            ev_t2[j] = sc.op("v", lambda v, sl=sl: v.tensor_tensor(
                bslc(O_RSBs(sl), 0, 128, 2048), bslc(O_QS, 0, 128, 2048),
                bslc(O_CS_S, 0, 128, 2048), ALU.mult),
                waits=[ev_qsev_ch[4 * j + 3]], inc=("v", 1))
            if j >= 1:
                po = (j - 1) % 3
                do = (O_Q + 4096 * (j - 1)) if j - 1 < 8 else (O_K + 4096 * (j - 9))
                ev_rope[j - 1] = sc.op("v", lambda v, do=do, po=po: v.tensor_tensor(
                    bslc(do, 0, 128, 2048), bslc(O_RSAs(po), 0, 128, 2048),
                    bslc(O_RSBs(po), 0, 128, 2048), ALU.add),
                    inc=("v", 1))

        ev_rope_all = sc.ev("v")
        ev_rope_all = sc.ev("v")

        if n_phases <= 1:
            return _finish(nc, sc, block, taps, tap_dram, AR, ARf,
                           [sc.ev("c"), sc.ev("v"), sc.ev("t")])

        ev_vmm = [None] * NTT
        ev_vevac = [None] * NTT
        ev_gevac = [None] * NTT
        for tt in range(NTT):
            pv = tt % 2
            Pv = PA[:, 1024 * pv:1024 * pv + 1024]
            Pg = PB[:, 512 * pv:512 * pv + 8]
            w0 = [EV_TPOSE]
            if tt >= 2:
                w0 += [ev_vevac[tt - 2], ev_gevac[tt - 2]]
            if tt == 0:
                w0 += [ev_qevac[14], ev_qevac[15]]
            ev = None
            for m in range(NM):
                lhs_off = O_XNT + 2 * 128 * (8 * tt + m)
                sc.op("t", lambda t, Pv=Pv, m=m, lhs_off=lhs_off: t.matmul(
                    Pv[:, 0:512], bslc(lhs_off, 0, 128, 128),
                    bap3(O_WQKVT + 2 * 128 * (8 * 16 + m), 1024, 4, 128),
                    start=(m == 0), stop=(m == NM - 1)),
                    waits=w0 if m == 0 else [])
                sc.op("t", lambda t, Pv=Pv, m=m, lhs_off=lhs_off: t.matmul(
                    Pv[:, 512:1024], bslc(lhs_off, 0, 128, 128),
                    bap3(O_WQKVT + 2 * 128 * (8 * 20 + m), 1024, 4, 128),
                    start=(m == 0), stop=(m == NM - 1)))
                ev = sc.op("t", lambda t, Pg=Pg, m=m, lhs_off=lhs_off: t.matmul(
                    Pg, bslc(lhs_off, 0, 128, 128), bslc(O_WGS + 2 * 8 * m, 0, 128, 8),
                    start=(m == 0), stop=(m == NM - 1)),
                    inc=("t", 1) if m == NM - 1 else None)
            ev_vmm[tt] = ev
            emit_qswap(tt)
            ev_vevac[tt] = sc.op("c", lambda c_, pv=pv, tt=tt: c_.activation(
                bass.AP(AR, (O_VAUG + 2080 * tt) // 2, [[NB2, 128], [65, 16], [1, 64]]),
                bass.AP(PA, 1024 * pv, [[2048, 128], [64, 16], [1, 64]]),
                AF.Identity),
                waits=[ev_vmm[tt]], inc=("c", 1))
            ev_gevac[tt] = sc.op("v", lambda v, Pg=Pg, tt=tt: v.tensor_copy(
                fslc(O_LS + 4 * 8 * tt, 0, 128, 8), Pg), waits=[ev_vmm[tt]], inc=("v", 1))
            sc.op("v", lambda v, tt=tt: v.memset(
                bass.AP(AR, (O_VAUG + 2080 * tt) // 2 + 64,
                        [[NB2, 128], [65, 16], [1, 1]]), 1.0),
                waits=[ev_vevac[tt]])
        po = 15 % 3
        vdummy()
        ev_rope[15] = sc.op("v", lambda v, po=po: v.tensor_tensor(
            bslc(O_K + 4096 * 7, 0, 128, 2048), bslc(O_RSAs(po), 0, 128, 2048),
            bslc(O_RSBs(po), 0, 128, 2048), ALU.add),
            inc=("v", 1))
        ev_rope_all = sc.ev("v")
        ev_vaug_all = sc.ev("v")


        GTs = lambda a, n: fslc(O_GT + 4 * a, 0, 128, n)
        L3 = fap(O_LS, [[8, 16], [1, 8]])
        sc.op("v", lambda v: v.reduce_max(ST(20, 16), L3, axis=AX.X))
        vdummy()
        m1b = fap(O_STATS + 4 * 20, [[1, 16], [0, 8]])
        sc.op("v", lambda v: v.tensor_tensor(GTs(0, 128), L3, m1b, ALU.is_ge))
        vdummy()
        sc.op("v", lambda v: v.scalar_tensor_tensor(
            GTs(128, 128), GTs(0, 128), -1e30, fslc(O_LS, 0, 128, 128),
            ALU.mult, ALU.add))
        vdummy()
        LM3 = fap(O_GT + 4 * 128, [[8, 16], [1, 8]])
        sc.op("v", lambda v: v.reduce_max(ST(36, 16), LM3, axis=AX.X))
        vdummy()
        m2b = fap(O_STATS + 4 * 36, [[1, 16], [0, 8]])
        sc.op("v", lambda v: v.tensor_tensor(GTs(256, 128), LM3, m2b, ALU.is_ge))
        ev_g2 = sc.op("v", lambda v: v.tensor_tensor(
            GTs(384, 16), ST(36, 16), ST(20, 16), ALU.subtract),
            inc=("v", 1))
        ev_g3 = sc.op("c", lambda c_: c_.activation(
            GTs(400, 16), GTs(384, 16), AF.Exp), waits=[ev_g2], inc=("c", 1))
        sc.op("v", lambda v: v.tensor_scalar(
            GTs(416, 16), GTs(400, 16), 1.0, None, ALU.add), waits=[ev_g3])
        vdummy()
        sc.op("v", lambda v: v.reciprocal(GTs(432, 16), GTs(416, 16)))
        vdummy()
        sc.op("v", lambda v: v.tensor_tensor(
            GTs(448, 16), GTs(400, 16), GTs(432, 16), ALU.mult))
        p1b = fap(O_GT + 4 * 432, [[1, 16], [0, 8]])
        p2b = fap(O_GT + 4 * 448, [[1, 16], [0, 8]])
        G13 = fap(O_GT, [[8, 16], [1, 8]])
        G23 = fap(O_GT + 4 * 256, [[8, 16], [1, 8]])
        sc.op("v", lambda v: v.tensor_tensor(GTs(0, 128), G13, p1b, ALU.mult))
        sc.op("v", lambda v: v.tensor_tensor(GTs(256, 128), G23, p2b, ALU.mult))
        vdummy()
        ev_gate = sc.op("v", lambda v: v.tensor_tensor(
            fslc(O_GS, 0, 128, 128), GTs(0, 128), GTs(256, 128), ALU.add),
            inc=("v", 1))

        if n_phases <= 2:
            return _finish(nc, sc, block, taps, tap_dram, AR, ARf,
                           [ev_gate, ev_vaug_all, sc.ev("c")])

        sc.op("v", lambda v: v.memset(bslc(O_INVRZ, 0, 128, 512), 0.0))
        ev_hmz = sc.op("v", lambda v: v.memset(bslc(O_INVRZ + 1024, 0, 128, 512), 0.0),
                       inc=("v", 1))
        ev_m4 = sc.op("g", lambda g: g.dma_start(bslc(O_M4, 0, 128, 2048), mlog.ap()),
                      waits=[ev_rope_all, ev_gate], inc=("g", 16))
        ev_exp_bank = {}
        ev_repmm = {}
        ev_invb = {}
        ev_tt = {}
        ev_av = {}
        ev_inv = {}
        ev_shdma = {}
        ev_expavWAR = {}

        def esb_off(par):
            return O_ESB if par == 0 else O_ESBB

        def emit_rep(n):
            if n < 0 or n not in ev_inv or n in ev_repmm:
                return
            par = n % 2
            Prep = PA[:, 512 * par:512 * par + 512]
            w = [ev_inv[n]]
            if (n - 2) in ev_invb:
                w.append(ev_invb[n - 2])
            ev_repmm[n] = sc.op("t", lambda t, Prep=Prep, par=par: t.matmul(
                Prep, bslc(O_CONES, 0, 128, 128),
                bslc(O_INVRZ + 1024 * par, 0, 128, 512),
                start=True, stop=True),
                waits=w, inc=("t", 1))

        def emit_invb(n):
            if n < 0 or n not in ev_repmm or n in ev_invb:
                return
            par = n % 2
            Prep = PA[:, 512 * par:512 * par + 512]
            w = [ev_repmm[n]]
            if (n - 2) in ev_tt:
                w.append(ev_tt[n - 2])
            ev_invb[n] = sc.op("v", lambda v, Prep=Prep, par=par: v.tensor_copy(
                bslc(O_INVB + 1024 * par, 0, 64, 512), Prep[0:64, :]),
                waits=w, inc=("v", 1))

        def emit_tt(n):
            if n < 0 or n not in ev_invb or n in ev_tt:
                return
            par = n % 2
            Pav = PB[:, 512 * par:512 * par + 512]
            h = (n // 8) * 2 + par
            cc = (n // 2) % 4
            ao_col = ao_off(h // 2) // 2 + 512 * cc
            if par == 0:
                ev_tt[n] = sc.op("v", lambda v, Pav=Pav, ao_col=ao_col: v.tensor_tensor(
                    AR[0:64, ao_col:ao_col + 512], Pav[0:64, :],
                    bslc(O_INVB, 0, 64, 512), ALU.mult),
                    waits=[ev_invb[n]], inc=("v", 1))
            else:
                sl = (n // 2) % 2
                w = [ev_invb[n]]
                if (n - 4) in ev_shdma:
                    w.append(ev_shdma[n - 4])
                ev_tt[n] = sc.op("v", lambda v, Pav=Pav, sl=sl: v.tensor_tensor(
                    bslc(O_TMP + 1024 * sl, 0, 64, 512), Pav[0:64, :],
                    bslc(O_INVB + 1024, 0, 64, 512), ALU.mult),
                    waits=w, inc=("v", 1))

        def emit_shdma(n):
            if n < 0 or n % 2 == 0 or n not in ev_tt or n in ev_shdma:
                return
            h = (n // 8) * 2 + 1
            cc = (n // 2) % 4
            ao_col = ao_off(h // 2) // 2 + 512 * cc
            sl = (n // 2) % 2
            ev_shdma[n] = sc.op("g", lambda g, ao_col=ao_col, sl=sl: g.dma_start(
                AR[64:128, ao_col:ao_col + 512], bslc(O_TMP + 1024 * sl, 0, 64, 512)),
                waits=[ev_tt[n]], inc=("g", 16))

        for b in range(nblocks):
            p, c = b // 4, b % 4
            nA, nB = 2 * b, 2 * b + 1
            j = p
            qo = (O_Q + 4096 * j) // 2
            ko = (O_K + 4096 * j) // 2
            nkt = 4 * c + 4
            ev_exp_this = {0: {}, 1: {}}
            for kt in range(nkt):
                pb = kt % 2
                is_diag = (kt >= 4 * c)
                for par, hb, Pbank in ((0, 0, PB), (1, 64, PA)):
                    Pst = Pbank[:, 1024 + 512 * pb: 1536 + 512 * pb]
                    n = nA + par
                    w = []
                    if b == 0 and kt < 2:
                        w += [ev_vaug_all, ev_hmz]
                    if ev_exp_bank.get((par, pb)) is not None:
                        w.append(ev_exp_bank[(par, pb)])
                    ev_s = sc.op("t", lambda t, Pst=Pst, kt=kt, c=c, qo=qo, ko=ko,
                                 hb=hb: t.matmul(
                        Pst,
                        AR[hb:hb + 64, ko + 128 * kt: ko + 128 * kt + 128],
                        AR[hb:hb + 64, qo + 512 * c: qo + 512 * c + 512],
                        start=True, stop=True),
                        waits=w, inc=("t", 1))
                    wv = [ev_s]
                    if (n - 2) in ev_expavWAR:
                        wv.append(ev_expavWAR[n - 2])
                    if b == 0 and par == 1 and kt == 0:
                        wv.append(ev_gate)
                    eoff = esb_off(par) + 1024 * kt
                    ev_e = sc.op("c", lambda c_, Pst=Pst, eoff=eoff:
                        c_.activation(bslc(eoff, 0, 128, 512), Pst, AF.Exp),
                        waits=wv, inc=("c", 1))
                    ev_exp_bank[(par, pb)] = ev_e
                    ev_exp_this[par][kt] = ev_e
            emit_rep(nA - 2)
            emit_rep(nB - 2)
            emit_shdma(nB - 4)
            ev_mask = {}
            for par in (0, 1):
                w = [ev_exp_this[par][kt] for kt in range(4 * c, 4 * c + 4)]
                if b == 0:
                    w.append(ev_m4)
                ev_mask[par] = sc.op("v", lambda v, par=par, c=c: v.tensor_tensor(
                    bslc(esb_off(par) + 4096 * c, 0, 128, 2048),
                    bslc(esb_off(par) + 4096 * c, 0, 128, 2048),
                    bslc(O_M4, 0, 128, 2048), ALU.mult),
                    waits=w, inc=("v", 1))
            emit_invb(nA - 2)
            emit_invb(nB - 2)
            emit_tt(nA - 2)
            emit_tt(nB - 2)
            for par in (0, 1):
                n = nA + par
                h = 2 * p + par
                Pav = PB[:, 512 * par:512 * par + 512]
                ev_a = None
                for kt in range(nkt):
                    w = [ev_mask[par]] if kt >= 4 * c else [ev_exp_this[par][kt]]
                    if kt == 0:
                        if (n - 2) in ev_tt:
                            w.append(ev_tt[n - 2])
                        if b == 0:
                            w.append(ev_vaug_all)
                    ev_a = sc.op("t", lambda t, Pav=Pav, kt=kt, h=h, par=par: t.matmul(
                        Pav[0:65, :],
                        AR[0:128, (O_VAUG + 2080 * kt) // 2 + 65 * h:
                                  (O_VAUG + 2080 * kt) // 2 + 65 * h + 65],
                        bslc(esb_off(par) + 1024 * kt, 0, 128, 512),
                        start=(kt == 0), stop=(kt == nkt - 1)),
                        waits=w, inc=("t", 1) if kt == nkt - 1 else None)
                ev_av[n] = ev_a
                ev_expavWAR[n] = ev_a
            for par in (0, 1):
                n = nA + par
                wv = [ev_av[n]]
                if (n - 2) in ev_repmm:
                    wv.append(ev_repmm[n - 2])

                def _recip(v, par=par):
                    with nc.allow_low_precision(reason="bf16 matmul operand"):
                        return v.reciprocal(
                            bslc(O_INVRZ + 1024 * par, 64, 1, 512),
                            PB[64:65, 512 * par:512 * par + 512])
                ev_inv[n] = sc.op("v", _recip, waits=wv, inc=("v", 1))
        for n in range(2 * nblocks - 2, 2 * nblocks + 6):
            emit_rep(n - 2)
            emit_invb(n - 2)
            emit_tt(n - 2)
            emit_shdma(n - 3)
        ev_heads_t = sc.ev("t")
        ev_heads_v = sc.ev("v")
        ev_heads_c = sc.ev("c")
        ev_heads_g = sc.ev("g")

        if n_phases <= 3:
            return _finish(nc, sc, block, taps, tap_dram, AR, ARf,
                           [ev_heads_t, ev_heads_v, ev_heads_c])

        ev_opevac = [None] * NTT
        for tt in range(NTT):
            par = tt % 2
            ev_x2 = sc.op("s", lambda s, tt=tt, par=par: s.dma_start(
                fslc(O_X2 + 4096 * par, 0, 128, DM),
                x_in[128 * tt:128 * tt + 128, :]),
                waits=[EV_TPOSE, ev_heads_t]
                      + ([ev_opevac[tt - 2]] if tt >= 2 else []),
                inc=("s", 16))
            Pop = (PA if par == 0 else PB)[:, 0:1024]
            w0 = [ev_heads_v, ev_heads_c, ev_heads_g]
            if tt >= 2:
                w0.append(ev_opevac[tt - 2])
            ev = None
            for m in range(NM):
                for cc in range(2):
                    ev = sc.op("t", lambda t, Pop=Pop, m=m, cc=cc, tt=tt: t.matmul(
                        Pop[:, 512 * cc:512 * cc + 512],
                        bslc(ao_off(m) + 2 * 128 * tt, 0, 128, 128),
                        bap3(O_WOT + 2 * 128 * (32 * cc + m), 1024, 4, 128),
                        start=(m == 0), stop=(m == NM - 1)),
                        waits=w0 if (m == 0 and cc == 0) else [],
                        inc=("t", 1) if (m == NM - 1 and cc == 1) else None)
            ev_opevac[tt] = sc.op("v", lambda v, Pop=Pop, tt=tt, par=par:
                v.tensor_tensor(
                    fslc(oa_off(tt), 0, 128, DM), Pop,
                    fslc(O_X2 + 4096 * par, 0, 128, DM), ALU.add),
                waits=[ev, ev_x2], inc=("v", 1))
        ev_op_all_v = sc.ev("v")

        if n_phases <= 4:
            return _finish(nc, sc, block, taps, tap_dram, AR, ARf, [ev_op_all_v])

        ev_w1 = {}
        ev_w3 = {}
        ev_w2last = [None] * NE
        ev_h1mm = {}
        ev_h3mm = {}
        ev_silu = {}
        ev_hh = {}
        ev_yevac = {}
        ev_yphase_end = [None] * NE

        def dma_w13(e, ht, waits1, waits3):
            par = ht % 2
            ev_w1[(e, ht)] = sc.op("g", lambda g, e=e, ht=ht, par=par: g.dma_start(
                bap(O_W1B + 2048 * par, [[128, 8], [1, 128]]),
                bass.AP(w1, DM * HFF * e + 128 * ht,
                        [[HFF, 128], [128 * HFF, 8], [1, 128]])),
                waits=waits1, inc=("g", 16))
            ev_w3[(e, ht)] = sc.op("g", lambda g, e=e, ht=ht, par=par: g.dma_start(
                bap(O_W3B + 2048 * par, [[128, 8], [1, 128]]),
                bass.AP(w3, DM * HFF * e + 128 * ht,
                        [[HFF, 128], [128 * HFF, 8], [1, 128]])),
                waits=waits3, inc=("g", 16))

        def dma_w2(e, waits):
            evw2 = None
            for ht in range(16):
                evw2 = sc.op("g", lambda g, e=e, ht=ht: g.dma_start(
                    bslc(w2_off(ht), 0, 128, 1024),
                    w2[e, 128 * ht:128 * ht + 128, :]),
                    waits=waits if ht == 0 else [], inc=("g", 16))
            ev_w2last[e] = evw2

        dma_w13(0, 0, [EV_TPOSE, ev_op_all_v], [])
        dma_w13(0, 1, [], [])
        dma_w2(0, [ev_op_all_v])

        for e in range(NE):
            for ht in range(16):
                par = ht % 2
                w1w = [ev_w1[(e, ht)]]
                if (e, ht - 1) in ev_silu:
                    w1w.append(ev_silu[(e, ht - 1)])
                elif e > 0:
                    w1w.append(ev_silu[(e - 1, 15)])
                if ht == 0:
                    if e == 0:
                        w1w.append(ev_op_all_v)
                    else:
                        w1w.append(ev_yevac[(e - 1, 14)])
                for m in range(NM):
                    for cc in range(4):
                        sc.op("t", lambda t, m=m, cc=cc, par=par: t.matmul(
                            PA[:, 512 * cc:512 * cc + 512],
                            bslc(O_W1B + 2048 * par + 2 * 128 * m, 0, 128, 128),
                            bap3(O_XNT + 2 * 128 * (32 * cc + m), 1024, 4, 128),
                            start=(m == 0), stop=(m == NM - 1)),
                            waits=w1w if (m == 0 and cc == 0) else [],
                            inc=("t", 1) if (m == NM - 1 and cc == 3) else None)
                ev_h1mm[(e, ht)] = sc.ev("t")
                w3w = [ev_w3[(e, ht)]]
                if (e, ht - 1) in ev_hh:
                    w3w.append(ev_hh[(e, ht - 1)])
                elif e > 0:
                    w3w.append(ev_hh[(e - 1, 15)])
                if ht == 0 and e > 0:
                    w3w.append(ev_yevac[(e - 1, 15)])
                for m in range(NM):
                    for cc in range(4):
                        sc.op("t", lambda t, m=m, cc=cc, par=par: t.matmul(
                            PB[:, 512 * cc:512 * cc + 512],
                            bslc(O_W3B + 2048 * par + 2 * 128 * m, 0, 128, 128),
                            bap3(O_XNT + 2 * 128 * (32 * cc + m), 1024, 4, 128),
                            start=(m == 0), stop=(m == NM - 1)),
                            waits=w3w if (m == 0 and cc == 0) else [],
                            inc=("t", 1) if (m == NM - 1 and cc == 3) else None)
                ev_h3mm[(e, ht)] = sc.ev("t")
                w = [ev_h1mm[(e, ht)]]
                if (e, ht - 1) in ev_hh:
                    w.append(ev_hh[(e, ht - 1)])
                elif e > 0:
                    w.append(ev_hh[(e - 1, 15)])
                sc.op("c", lambda c_: c_.activation(
                    bslc(O_S1, 0, 128, 1024), PA[:, 0:1024], AF.Silu),
                    waits=w)
                ev_silu[(e, ht)] = sc.op("c", lambda c_: c_.activation(
                    bslc(O_S1 + 2048, 0, 128, 1024), PA[:, 1024:2048], AF.Silu),
                    inc=("c", 1))
                ev_hh[(e, ht)] = sc.op("v", lambda v, ht=ht: v.scalar_tensor_tensor(
                    bslc(hh_off(ht), 0, 128, 2048), PB[:, :], 1.0,
                    bslc(O_S1, 0, 128, 2048), ALU.mult, ALU.mult),
                    waits=[ev_h3mm[(e, ht)], ev_silu[(e, ht)]], inc=("v", 1))
                if ht + 2 < 16:
                    dma_w13(e, ht + 2, [ev_h1mm[(e, ht)]], [ev_h3mm[(e, ht)]])
                elif ht == 15 and e + 1 < NE:
                    dma_w13(e + 1, 0, [ev_h1mm[(e, 14)]], [ev_h3mm[(e, 14)]])
                    dma_w13(e + 1, 1, [ev_h1mm[(e, 15)]], [ev_h3mm[(e, 15)]])
            for tt in range(NTT):
                par = tt % 2
                Py = (PA if par == 0 else PB)[:, 0:1024]
                w0 = [ev_hh[(e, 15)], ev_w2last[e]]
                if tt >= 2:
                    w0.append(ev_yevac[(e, tt - 2)])
                ev = None
                for ht in range(16):
                    for cc in range(2):
                        ev = sc.op("t", lambda t, Py=Py, ht=ht, cc=cc, tt=tt: t.matmul(
                            Py[:, 512 * cc:512 * cc + 512],
                            bslc(hh_off(ht) + 2 * 128 * tt, 0, 128, 128),
                            bslc(w2_off(ht) + 2 * 512 * cc, 0, 128, 512),
                            start=(ht == 0), stop=(ht == 15)),
                            waits=w0 if (ht == 0 and cc == 0) else [],
                            inc=("t", 1) if (ht == 15 and cc == 1) else None)
                sc.op("v", lambda v, Py=Py, tt=tt, e=e: v.scalar_tensor_tensor(
                    fslc(oa_off(tt), 0, 128, 512), Py[:, 0:512],
                    fslc(O_GS + 4 * (8 * tt + e), 0, 128, 1),
                    fslc(oa_off(tt), 0, 128, 512), ALU.mult, ALU.add),
                    waits=[ev, ev_gate])
                ev_yevac[(e, tt)] = sc.op("v", lambda v, Py=Py, tt=tt, e=e:
                    v.scalar_tensor_tensor(
                        fslc(oa_off(tt) + 2048, 0, 128, 512), Py[:, 512:1024],
                        fslc(O_GS + 4 * (8 * tt + e), 0, 128, 1),
                        fslc(oa_off(tt) + 2048, 0, 128, 512),
                        ALU.mult, ALU.add),
                    inc=("v", 1))
            ev_yphase_end[e] = ev_yevac[(e, NTT - 1)]
            if e + 1 < NE:
                dma_w2(e + 1, [ev_yphase_end[e]])

        for tt in range(NTT):
            sc.op("s", lambda s, tt=tt: s.dma_start(
                out[128 * tt:128 * tt + 128, :], fslc(oa_off(tt), 0, 128, DM)),
                waits=[ev_yevac[(NE - 1, tt)]], inc=("s", 16))
        return _finish(nc, sc, block, taps, tap_dram, AR, ARf, [])


def _finish(nc, sc, block, taps, tap_dram, AR, ARf, final_evs):
    for (name, rows, cols, dt, off) in taps:
        src = AR if dt == BF16 else ARf
        div = 2 if dt == BF16 else 4
        sc.op("s", lambda s, name=name, off=off, rows=rows, cols=cols, src=src,
              div=div: s.dma_start(
                  tap_dram[name].ap(), src[0:rows, off // div: off // div + cols]),
              waits=[sc.ev(k) for k in ("t", "v", "c")] + list(final_evs),
              inc=("s", 16))
    sc.op("s", lambda s: None, waits=[("s", sc.cnt["s"])])

    @block.tensor
    def _(t):
        sc.emit("t", t)

    @block.vector
    def _(v):
        sc.emit("v", v)

    @block.scalar
    def _(c):
        sc.emit("c", c)

    @block.gpsimd
    def _(g):
        sc.emit("g", g)

    @block.sync
    def _(s):
        sc.emit("s", s)
    return nc


def _host_consts():
    inv_freq = (10000.0 ** (-np.arange(0, DK, 2, dtype=np.float32) / DK))
    t = np.arange(S, dtype=np.float32)
    ang = np.outer(inv_freq, t)
    cosT = np.cos(ang).astype(np.float32)
    sinT = np.sin(ang).astype(np.float32)
    C128 = np.zeros((128, S), np.float32)
    S128 = np.zeros((128, S), np.float32)
    for p in range(128):
        i = (p % 64) // 2
        C128[p] = cosT[i]
        S128[p] = sinT[i] * (-1.0 if p % 2 == 0 else 1.0)
    PSW = np.zeros((128, 128), np.float32)
    for p in range(128):
        PSW[p ^ 1, p] = 1.0
    PSH = np.zeros((128, 128), np.float32)
    for kk in range(64):
        PSH[kk, kk + 64] = 1.0
    M = np.ones((128, 2048), dtype=np.float32)
    for jp in range(4):
        blk = M[:, 512 * jp:512 * jp + 512]
        blk[:, :128 * jp] = 0.0
        for kk in range(128):
            blk[kk, 128 * jp:128 * jp + kk] = 0.0
    I = np.eye(128, dtype=np.float32)
    cast = lambda a: np.ascontiguousarray(a.astype(bfdt))
    return dict(ctab=cast(C128), stab=cast(S128), psw=cast(PSW), psh=cast(PSH),
                mlog=cast(M), ident=cast(I))


def kernel(L0_attn_x, L0_attn_W_qkv, L0_attn_W_o, L0_ffn_w_gate,
           L0_ffn_w1, L0_ffn_w3, L0_ffn_w2, _n_phases=99, _taps=(), _nc=None,
           _nblocks=32):
    X = np.ascontiguousarray(np.asarray(L0_attn_x, dtype=np.float32))
    base = dict(wqkv=np.ascontiguousarray(np.asarray(L0_attn_W_qkv, np.float32)),
                wo=np.ascontiguousarray(np.asarray(L0_attn_W_o, np.float32)),
                wg=np.ascontiguousarray(np.asarray(L0_ffn_w_gate, np.float32)),
                w1=np.ascontiguousarray(np.asarray(L0_ffn_w1, np.float32)),
                w3=np.ascontiguousarray(np.asarray(L0_ffn_w3, np.float32)),
                w2=np.ascontiguousarray(np.asarray(L0_ffn_w2, np.float32)),
                **_host_consts())
    nc = _nc if _nc is not None else build(n_phases=_n_phases, taps=_taps, nblocks=_nblocks)
    in_maps = [dict(base, x=np.ascontiguousarray(X[b])) for b in range(8)]
    res = run_bass_kernel_spmd(nc, in_maps, core_ids=list(range(8)))
    if _taps:
        return res
    return np.stack([np.asarray(res.results[b]["out"]) for b in range(8)], axis=0)


# version 11
# speedup vs baseline: 1.1892x; 1.1892x over previous
import numpy as np
import ml_dtypes
import concourse.bass as bass
import concourse.mybir as mybir
from concourse.bass_utils import run_bass_kernel_spmd

F32 = mybir.dt.float32
BF16 = mybir.dt.bfloat16
AF = mybir.ActivationFunctionType
ALU = mybir.AluOpType
AX = mybir.AxisListType
bfdt = ml_dtypes.bfloat16

S = 2048
DM = 1024
NH = 16
DK = 64
HFF = 2048
NE = 8
NM = DM // 128
NTT = S // 128

NBYTES = 212352
NB2 = NBYTES // 2
NB4 = NBYTES // 4

O_XNT   = 0
O_XST2  = 0
O_LNW   = 8192
O_WOT   = 32768
O_W1B   = 32768
O_W3B   = 36864
O_S1    = 40960
O_W2_A  = 45056
O_WQKVT = 49152
O_ESB   = 49152
O_X2    = 49152
O_W2_B  = 49152
O_AO    = 65536
O_HH_HI = 106496
O_CS_C  = 98304
O_CS_S  = 102400
O_INVRZ = 98304
O_INVB  = 100352
O_TMP   = 102400
O_M4    = 94208
O_ESBB  = 69632


def ao_off(j):
    return 205312 if j == 0 else O_Q + 4096 * (j - 1)
O_W2_C  = 98304
O_Q     = 106496
O_K     = 139264
O_STAGE = 106496
def oa_off(tt):
    return 65536 + 4096 * tt if tt < 8 else 139264 + 4096 * (tt - 8)
O_VAUG  = 172032
O_XNST  = 172032
O_QS    = 49152
def O_RSAs(sl):
    return 53248 + 4096 * sl
def O_RSBs(sl):
    return 65536 + 4096 * sl
O_RS1   = 205312
O_W2_D  = 205312
O_HH_LO = O_VAUG
O_CPSW  = 209408
O_CPSH  = 209664
O_CMLOG = 209920
O_CID   = 210944
O_CONES = 211200
O_WGS   = 211456
O_STATS = 211584
O_GS    = 211840
O_XST   = 65536
O_LS    = 77824
O_GT    = 78336


def w2_off(ht):
    if ht < 8:
        return O_W2_B + 2048 * ht
    if ht < 12:
        return O_W2_C + 2048 * (ht - 8)
    if ht < 14:
        return O_W2_A + 2048 * (ht - 12)
    return O_W2_D + 2048 * (ht - 14)


def hh_off(ht):
    return (O_HH_LO + 4096 * ht) if ht < 8 else (O_HH_HI + 4096 * (ht - 8))


class Sched:
    def __init__(self, sems):
        self.sems = sems
        self.cnt = {k: 0 for k in sems}
        self.ops = []
        self.hiwater = {}

    def ev(self, sem):
        return (sem, self.cnt[sem])

    def op(self, eng, emit, waits=(), inc=None):
        ws = []
        for w in waits:
            if w is None:
                continue
            sname, val = w
            if val <= 0:
                continue
            key = (eng, sname)
            if self.hiwater.get(key, 0) >= val:
                continue
            self.hiwater[key] = val
            ws.append((sname, val))
        if inc is not None:
            self.cnt[inc[0]] += inc[1]
        self.ops.append((eng, ws, emit, inc))
        return (inc[0], self.cnt[inc[0]]) if inc is not None else None

    def emit(self, eng_name, engine):
        for (e, ws, fn, inc) in self.ops:
            if e != eng_name:
                continue
            for (sname, val) in ws:
                engine.wait_ge(self.sems[sname], val)
            inst = fn(engine)
            if inc is not None:
                inst.then_inc(self.sems[inc[0]], inc[1])


def build(n_phases=99, taps=(), nblocks=32):
    nc = bass.Bass("TRN2", target_bir_lowering=False)
    x_in  = nc.dram_tensor("x",    [S, DM],        F32, kind="ExternalInput")
    wqkv  = nc.dram_tensor("wqkv", [3 * DM, DM],   F32, kind="ExternalInput")
    wo    = nc.dram_tensor("wo",   [DM, DM],       F32, kind="ExternalInput")
    wg    = nc.dram_tensor("wg",   [DM, NE],       F32, kind="ExternalInput")
    w1    = nc.dram_tensor("w1",   [NE, DM, HFF],  F32, kind="ExternalInput")
    w3    = nc.dram_tensor("w3",   [NE, DM, HFF],  F32, kind="ExternalInput")
    w2    = nc.dram_tensor("w2",   [NE, HFF, DM],  F32, kind="ExternalInput")
    ctab  = nc.dram_tensor("ctab", [128, S],   BF16, kind="ExternalInput")
    stab  = nc.dram_tensor("stab", [128, S],   BF16, kind="ExternalInput")
    psw   = nc.dram_tensor("psw",  [128, 128], BF16, kind="ExternalInput")
    psh   = nc.dram_tensor("psh",  [128, 128], BF16, kind="ExternalInput")
    mlog  = nc.dram_tensor("mlog", [128, 2048], BF16, kind="ExternalInput")
    ident = nc.dram_tensor("ident", [128, 128], BF16, kind="ExternalInput")
    out   = nc.dram_tensor("out",  [S, DM],        F32, kind="ExternalOutput")
    tap_dram = {}
    for (name, rows, cols, dt, off) in taps:
        tap_dram[name] = nc.dram_tensor("tap_" + name, [rows, cols],
                                        dt, kind="ExternalOutput")

    import contextlib
    with contextlib.ExitStack() as stack:
        ec = stack.enter_context
        AR = ec(nc.sbuf_tensor("AR", [128, NB2], BF16))
        PA = ec(nc.psum_tensor("PA", [128, 2048], F32))
        PB = ec(nc.psum_tensor("PB", [128, 2048], F32))
        _sG = ec(nc.semaphore("sG"))
        _sS = ec(nc.semaphore("sS"))
        _sT = ec(nc.semaphore("sT"))
        _sV = ec(nc.semaphore("sV"))
        _sC = ec(nc.semaphore("sC"))
        block = ec(nc.Block())
        ARf = AR.bitcast(F32)
        sems = {"g": _sG, "s": _sS, "t": _sT, "v": _sV, "c": _sC}
        sc = Sched(sems)

        def bslc(byte_off, p0, pn, cols):
            return AR[p0:p0 + pn, byte_off // 2: byte_off // 2 + cols]

        def bap(byte_off, dims):
            return bass.AP(AR, byte_off // 2, [[NB2, 128]] + dims)

        def fslc(byte_off, p0, pn, cols):
            return ARf[p0:p0 + pn, byte_off // 4: byte_off // 4 + cols]

        def fap(byte_off, dims):
            return bass.AP(ARf, byte_off // 4, [[NB4, 128]] + dims)

        def bap3(byte_off, step, n, inner):
            return bass.AP(AR, byte_off // 2, [[NB2, 128], [step, n], [1, inner]])

        def ST(col, n=1, p0=0, pn=128):
            return fslc(O_STATS + 4 * col, p0, pn, n)

        ap_eps = fslc(O_STATS + 4 * 18, 0, 128, 1)
        ap_zero = fslc(O_STATS + 4 * 19, 0, 128, 1)
        nc.const_aps.register(F32, 1e-5, ap_eps)
        nc.const_aps.register(F32, 0.0, ap_zero)

        CPSW = bslc(O_CPSW, 0, 128, 128)

        sc.op("g", lambda g: g.dma_start(CPSW, psw.ap()), inc=("g", 16))
        sc.op("g", lambda g: g.dma_start(bslc(O_CS_C, 0, 128, S), ctab.ap()), inc=("g", 16))
        sc.op("g", lambda g: g.dma_start(bslc(O_CS_S, 0, 128, S), stab.ap()), inc=("g", 16))
        for m in range(NM):
            sc.op("g", lambda g, m=m: g.dma_start(
                bslc(O_WGS + 2 * 8 * m, 0, 128, 8),
                wg[128 * m:128 * m + 128, :]), inc=("g", 16))
        for j in range(24):
            sc.op("g", lambda g, j=j: g.dma_start(
                bslc(O_STAGE + 2048 * j, 0, 128, DM),
                wqkv[128 * j:128 * j + 128, :]), inc=("g", 16))
        for j in range(8):
            sc.op("g", lambda g, j=j: g.dma_start(
                bslc(O_STAGE + 2048 * (24 + j), 0, 128, DM),
                wo[128 * j:128 * j + 128, :]), inc=("g", 16))
        ev_gload = sc.ev("g")
        sc.op("v", lambda v: v.memset(bslc(O_CONES, 0, 128, 128), 1.0))
        sc.op("v", lambda v: v.memset(fslc(O_STATS + 4 * 18, 0, 128, 1), 1e-5))
        ev_consts = sc.op("v", lambda v: v.memset(
            fslc(O_STATS + 4 * 19, 0, 128, 1), 0.0), inc=("v", 1))

        LNW = O_LNW

        def LW(col, n=1):
            return fslc(LNW + 4 * col, 0, 128, n)

        ev_rs = [None] * NTT
        ev_sq = [None] * NTT
        ev_xload = [None] * NTT
        for tt in range(NTT):
            par = tt % 2
            xs = fslc(O_XST2 + 4 * DM * par, 0, 128, DM)
            w = []
            if tt >= 2:
                w = [ev_rs[tt - 2], ev_sq[tt - 2]]
            ev_xload[tt] = sc.op("s", lambda s, tt=tt, xs=xs: s.dma_start(
                xs, x_in[128 * tt:128 * tt + 128, :]), waits=w, inc=("s", 16))
            ev_rs[tt] = sc.op("v", lambda v, xs=xs, tt=tt: v.reduce_sum(
                LW(tt), xs, axis=AX.X), waits=[ev_xload[tt]], inc=("v", 1))
            ev_sq[tt] = sc.op("c", lambda c, xs=xs, tt=tt, par=par: c.activation(
                bslc(O_RS1 + 2048 * par, 0, 128, 1024), xs, AF.Square,
                accum_out=LW(16 + tt)),
                waits=[ev_xload[tt], ev_consts], inc=("c", 1))
        ev_rs_all = sc.ev("v")
        ev_sq_all = sc.ev("c")

        def vdummy():
            sc.op("v", lambda v: v.memset(fslc(O_STATS + 4 * 62, 0, 128, 1), 0.0))

        def cdummy():
            sc.op("c", lambda c: c.activation(
                fslc(O_STATS + 4 * 61, 0, 128, 1),
                fslc(O_STATS + 4 * 19, 0, 128, 1), AF.Identity))

        sc.op("v", lambda v: v.tensor_scalar(
            LW(32, 16), LW(0, 16), 1.0 / DM, None, ALU.mult), waits=[ev_sq_all])
        vdummy()
        sc.op("v", lambda v: v.tensor_tensor(
            LW(48, 16), LW(32, 16), LW(32, 16), ALU.mult))
        vdummy()
        ev_var = sc.op("v", lambda v: v.scalar_tensor_tensor(
            LW(64, 16), LW(16, 16), 1.0 / DM, LW(48, 16), ALU.mult, ALU.subtract),
            inc=("v", 1))
        ev_ln = sc.op("c", lambda c: c.activation(
            LW(80, 16), LW(64, 16), AF.Ln, bias=1e-5),
            waits=[ev_var], inc=("c", 1))
        ev_lh = sc.op("v", lambda v: v.tensor_scalar(
            LW(96, 16), LW(80, 16), -0.5, None, ALU.mult),
            waits=[ev_ln], inc=("v", 1))
        ev_rstd = sc.op("c", lambda c: c.activation(
            LW(112, 16), LW(96, 16), AF.Exp), waits=[ev_lh], inc=("c", 1))
        cdummy()
        ev_ngm = sc.op("v", lambda v: v.scalar_tensor_tensor(
            LW(128, 16), LW(32, 16), -1.0, LW(112, 16), ALU.mult, ALU.mult),
            waits=[ev_rstd], inc=("v", 1))
        ev_xn = [None] * NTT
        for tt in range(NTT):
            par = tt % 2
            xs = fslc(O_XST2 + 4 * DM * par, 0, 128, DM)
            w = [ev_rs_all, ev_sq_all] if tt < 2 else [ev_xn[tt - 2]]
            ev_x2l = sc.op("s", lambda s, tt=tt, xs=xs: s.dma_start(
                xs, x_in[128 * tt:128 * tt + 128, :]), waits=w, inc=("s", 16))
            ev_xn[tt] = sc.op("c", lambda c, xs=xs, tt=tt: c.activation(
                bslc(O_XNST + 2048 * tt, 0, 128, DM), xs, AF.Identity,
                bias=LW(128 + tt), scale=LW(112 + tt)),
                waits=[ev_x2l, ev_ngm], inc=("c", 1))
        ev_ln_all = sc.ev("c")
        ev_xload_all = sc.ev("s")

        sc.op("s", lambda s: s.dma_start(
            bass.AP(AR, O_WQKVT // 2, [[NB2, 128], [128, 192], [1, 128]]),
            bslc(O_STAGE, 0, 128, 24 * DM), transpose=True),
            waits=[ev_gload, ev_xload_all], inc=("s", 16))
        sc.op("s", lambda s: s.dma_start(
            bass.AP(AR, O_WOT // 2, [[NB2, 128], [128, 64], [1, 128]]),
            bslc(O_STAGE + 2048 * 24, 0, 128, 8 * DM), transpose=True),
            inc=("s", 16))
        sc.op("s", lambda s: s.dma_start(
            bass.AP(AR, O_XNT // 2, [[NB2, 128], [128, 128], [1, 128]]),
            bslc(O_XNST, 0, 128, 16 * DM), transpose=True),
            inc=("s", 16))
        EV_TPOSE = sc.ev("s")

        if n_phases <= 0:
            return _finish(nc, sc, block, taps, tap_dram, AR, ARf, [EV_TPOSE])

        ev_qkv = [None] * 16
        ev_qevac = [None] * 16
        ev_qsw = [None] * 16
        ev_qsevac = [None] * 16
        ev_rope = [None] * 16
        ev_t2 = [None] * 16
        dst_prev = None
        for j in range(16):
            P = PA if j % 2 == 0 else PB
            dst_off = (O_Q + 4096 * j) if j < 8 else (O_K + 4096 * (j - 8))
            w0 = [EV_TPOSE] + ([ev_qevac[j - 2]] if j >= 2 else [])
            ev = None
            for m in range(NM):
                for c in range(4):
                    ev = sc.op("t", lambda t, P=P, m=m, c=c, j=j: t.matmul(
                        P[:, 512 * c:512 * c + 512],
                        bslc(O_WQKVT + 2 * 128 * (8 * j + m), 0, 128, 128),
                        bap3(O_XNT + 2 * 128 * (32 * c + m), 1024, 4, 128),
                        start=(m == 0), stop=(m == NM - 1)),
                        waits=w0 if (m == 0 and c == 0) else [],
                        inc=("t", 1) if (m == NM - 1 and c == 3) else None)
            ev_qkv[j] = ev
            scl = 0.125 if j < 8 else 1.0
            ev_qevac[j] = sc.op("c", lambda c_, P=P, dst_off=dst_off, scl=scl:
                c_.activation(bslc(dst_off, 0, 128, 2048), P[:, :], AF.Identity,
                              scale=scl),
                waits=[ev_qkv[j]], inc=("c", 1))
        ev_qsev_ch = {}

        def emit_qswap(j):
            dst_off = (O_Q + 4096 * j) if j < 8 else (O_K + 4096 * (j - 8))
            for c in range(4):
                gidx = 4 * j + c
                pb = gidx % 2
                Psw = PB[:, 1024 + 512 * pb:1536 + 512 * pb]
                w = [ev_qevac[j]]
                if gidx >= 2:
                    w.append(ev_qsev_ch[gidx - 2])
                if j == 0 and c < 2:
                    w.append(ev_qevac[15])
                ev_m = sc.op("t", lambda t, Psw=Psw, c=c, dst_off=dst_off: t.matmul(
                    Psw, CPSW, bslc(dst_off + 1024 * c, 0, 128, 512),
                    start=True, stop=True),
                    waits=w, inc=("t", 1))
                ev_qsev_ch[gidx] = sc.op("c", lambda c_, Psw=Psw, c=c: c_.activation(
                    bslc(O_QS + 1024 * c, 0, 128, 512), Psw, AF.Identity),
                    waits=[ev_m] + ([ev_t2[j - 1]] if (j >= 1 and c == 0) else []),
                    inc=("c", 1))
            sl = j % 3
            sc.op("v", lambda v, dst_off=dst_off, sl=sl: v.tensor_tensor(
                bslc(O_RSAs(sl), 0, 128, 2048), bslc(dst_off, 0, 128, 2048),
                bslc(O_CS_C, 0, 128, 2048), ALU.mult),
                waits=[ev_qevac[j]] + ([ev_qevac[15]] if j == 0 else []))
            ev_t2[j] = sc.op("v", lambda v, sl=sl: v.tensor_tensor(
                bslc(O_RSBs(sl), 0, 128, 2048), bslc(O_QS, 0, 128, 2048),
                bslc(O_CS_S, 0, 128, 2048), ALU.mult),
                waits=[ev_qsev_ch[4 * j + 3]], inc=("v", 1))
            if j >= 1:
                po = (j - 1) % 3
                do = (O_Q + 4096 * (j - 1)) if j - 1 < 8 else (O_K + 4096 * (j - 9))
                ev_rope[j - 1] = sc.op("v", lambda v, do=do, po=po: v.tensor_tensor(
                    bslc(do, 0, 128, 2048), bslc(O_RSAs(po), 0, 128, 2048),
                    bslc(O_RSBs(po), 0, 128, 2048), ALU.add),
                    inc=("v", 1))

        ev_rope_all = sc.ev("v")
        ev_rope_all = sc.ev("v")

        if n_phases <= 1:
            return _finish(nc, sc, block, taps, tap_dram, AR, ARf,
                           [sc.ev("c"), sc.ev("v"), sc.ev("t")])

        ev_vmm = [None] * NTT
        ev_vevac = [None] * NTT
        ev_gevac = [None] * NTT
        for tt in range(NTT):
            pv = tt % 2
            Pv = PA[:, 1024 * pv:1024 * pv + 1024]
            Pg = PB[:, 512 * pv:512 * pv + 8]
            w0 = [EV_TPOSE]
            if tt >= 2:
                w0 += [ev_vevac[tt - 2], ev_gevac[tt - 2]]
            if tt == 0:
                w0 += [ev_qevac[14], ev_qevac[15]]
            ev = None
            for m in range(NM):
                lhs_off = O_XNT + 2 * 128 * (8 * tt + m)
                sc.op("t", lambda t, Pv=Pv, m=m, lhs_off=lhs_off: t.matmul(
                    Pv[:, 0:512], bslc(lhs_off, 0, 128, 128),
                    bap3(O_WQKVT + 2 * 128 * (8 * 16 + m), 1024, 4, 128),
                    start=(m == 0), stop=(m == NM - 1)),
                    waits=w0 if m == 0 else [])
                sc.op("t", lambda t, Pv=Pv, m=m, lhs_off=lhs_off: t.matmul(
                    Pv[:, 512:1024], bslc(lhs_off, 0, 128, 128),
                    bap3(O_WQKVT + 2 * 128 * (8 * 20 + m), 1024, 4, 128),
                    start=(m == 0), stop=(m == NM - 1)))
                ev = sc.op("t", lambda t, Pg=Pg, m=m, lhs_off=lhs_off: t.matmul(
                    Pg, bslc(lhs_off, 0, 128, 128), bslc(O_WGS + 2 * 8 * m, 0, 128, 8),
                    start=(m == 0), stop=(m == NM - 1)),
                    inc=("t", 1) if m == NM - 1 else None)
            ev_vmm[tt] = ev
            emit_qswap(tt)
            ev_vevac[tt] = sc.op("c", lambda c_, pv=pv, tt=tt: c_.activation(
                bass.AP(AR, (O_VAUG + 2080 * tt) // 2, [[NB2, 128], [65, 16], [1, 64]]),
                bass.AP(PA, 1024 * pv, [[2048, 128], [64, 16], [1, 64]]),
                AF.Identity),
                waits=[ev_vmm[tt]], inc=("c", 1))
            ev_gevac[tt] = sc.op("v", lambda v, Pg=Pg, tt=tt: v.tensor_copy(
                fslc(O_LS + 4 * 8 * tt, 0, 128, 8), Pg), waits=[ev_vmm[tt]], inc=("v", 1))
            sc.op("v", lambda v, tt=tt: v.memset(
                bass.AP(AR, (O_VAUG + 2080 * tt) // 2 + 64,
                        [[NB2, 128], [65, 16], [1, 1]]), 1.0),
                waits=[ev_vevac[tt]])
        po = 15 % 3
        vdummy()
        ev_rope[15] = sc.op("v", lambda v, po=po: v.tensor_tensor(
            bslc(O_K + 4096 * 7, 0, 128, 2048), bslc(O_RSAs(po), 0, 128, 2048),
            bslc(O_RSBs(po), 0, 128, 2048), ALU.add),
            inc=("v", 1))
        ev_rope_all = sc.ev("v")
        ev_vaug_all = sc.ev("v")


        GTs = lambda a, n: fslc(O_GT + 4 * a, 0, 128, n)
        L3 = fap(O_LS, [[8, 16], [1, 8]])
        sc.op("v", lambda v: v.reduce_max(ST(20, 16), L3, axis=AX.X))
        vdummy()
        m1b = fap(O_STATS + 4 * 20, [[1, 16], [0, 8]])
        sc.op("v", lambda v: v.tensor_tensor(GTs(0, 128), L3, m1b, ALU.is_ge))
        vdummy()
        sc.op("v", lambda v: v.scalar_tensor_tensor(
            GTs(128, 128), GTs(0, 128), -1e30, fslc(O_LS, 0, 128, 128),
            ALU.mult, ALU.add))
        vdummy()
        LM3 = fap(O_GT + 4 * 128, [[8, 16], [1, 8]])
        sc.op("v", lambda v: v.reduce_max(ST(36, 16), LM3, axis=AX.X))
        vdummy()
        m2b = fap(O_STATS + 4 * 36, [[1, 16], [0, 8]])
        sc.op("v", lambda v: v.tensor_tensor(GTs(256, 128), LM3, m2b, ALU.is_ge))
        ev_g2 = sc.op("v", lambda v: v.tensor_tensor(
            GTs(384, 16), ST(36, 16), ST(20, 16), ALU.subtract),
            inc=("v", 1))
        ev_g3 = sc.op("c", lambda c_: c_.activation(
            GTs(400, 16), GTs(384, 16), AF.Exp), waits=[ev_g2], inc=("c", 1))
        sc.op("v", lambda v: v.tensor_scalar(
            GTs(416, 16), GTs(400, 16), 1.0, None, ALU.add), waits=[ev_g3])
        vdummy()
        sc.op("v", lambda v: v.reciprocal(GTs(432, 16), GTs(416, 16)))
        vdummy()
        sc.op("v", lambda v: v.tensor_tensor(
            GTs(448, 16), GTs(400, 16), GTs(432, 16), ALU.mult))
        p1b = fap(O_GT + 4 * 432, [[1, 16], [0, 8]])
        p2b = fap(O_GT + 4 * 448, [[1, 16], [0, 8]])
        G13 = fap(O_GT, [[8, 16], [1, 8]])
        G23 = fap(O_GT + 4 * 256, [[8, 16], [1, 8]])
        sc.op("v", lambda v: v.tensor_tensor(GTs(0, 128), G13, p1b, ALU.mult))
        sc.op("v", lambda v: v.tensor_tensor(GTs(256, 128), G23, p2b, ALU.mult))
        vdummy()
        ev_gate = sc.op("v", lambda v: v.tensor_tensor(
            fslc(O_GS, 0, 128, 128), GTs(0, 128), GTs(256, 128), ALU.add),
            inc=("v", 1))

        if n_phases <= 2:
            return _finish(nc, sc, block, taps, tap_dram, AR, ARf,
                           [ev_gate, ev_vaug_all, sc.ev("c")])

        sc.op("v", lambda v: v.memset(bslc(O_INVRZ, 0, 128, 512), 0.0))
        ev_hmz = sc.op("v", lambda v: v.memset(bslc(O_INVRZ + 1024, 0, 128, 512), 0.0),
                       inc=("v", 1))
        ev_m4 = sc.op("g", lambda g: g.dma_start(bslc(O_M4, 0, 128, 2048), mlog.ap()),
                      waits=[ev_rope_all, ev_gate], inc=("g", 16))
        ev_exp_bank = {}
        ev_repmm = {}
        ev_invb = {}
        ev_tt = {}
        ev_av = {}
        ev_inv = {}
        ev_shdma = {}
        ev_expavWAR = {}

        def esb_off(par):
            return O_ESB if par == 0 else O_ESBB

        def emit_rep(n):
            if n < 0 or n not in ev_inv or n in ev_repmm:
                return
            par = n % 2
            Prep = PA[:, 512 * par:512 * par + 512]
            w = [ev_inv[n]]
            if (n - 2) in ev_invb:
                w.append(ev_invb[n - 2])
            ev_repmm[n] = sc.op("t", lambda t, Prep=Prep, par=par: t.matmul(
                Prep, bslc(O_CONES, 0, 128, 128),
                bslc(O_INVRZ + 1024 * par, 0, 128, 512),
                start=True, stop=True),
                waits=w, inc=("t", 1))

        def emit_invb(n):
            if n < 0 or n not in ev_repmm or n in ev_invb:
                return
            par = n % 2
            Prep = PA[:, 512 * par:512 * par + 512]
            w = [ev_repmm[n]]
            if (n - 2) in ev_tt:
                w.append(ev_tt[n - 2])
            ev_invb[n] = sc.op("v", lambda v, Prep=Prep, par=par: v.tensor_copy(
                bslc(O_INVB + 1024 * par, 0, 64, 512), Prep[0:64, :]),
                waits=w, inc=("v", 1))

        def emit_tt(n):
            if n < 0 or n not in ev_invb or n in ev_tt:
                return
            par = n % 2
            Pav = PB[:, 512 * par:512 * par + 512]
            h = (n // 8) * 2 + par
            cc = (n // 2) % 4
            ao_col = ao_off(h // 2) // 2 + 512 * cc
            if par == 0:
                ev_tt[n] = sc.op("v", lambda v, Pav=Pav, ao_col=ao_col: v.tensor_tensor(
                    AR[0:64, ao_col:ao_col + 512], Pav[0:64, :],
                    bslc(O_INVB, 0, 64, 512), ALU.mult),
                    waits=[ev_invb[n]], inc=("v", 1))
            else:
                sl = (n // 2) % 2
                w = [ev_invb[n]]
                if (n - 4) in ev_shdma:
                    w.append(ev_shdma[n - 4])
                ev_tt[n] = sc.op("v", lambda v, Pav=Pav, sl=sl: v.tensor_tensor(
                    bslc(O_TMP + 1024 * sl, 0, 64, 512), Pav[0:64, :],
                    bslc(O_INVB + 1024, 0, 64, 512), ALU.mult),
                    waits=w, inc=("v", 1))

        def emit_shdma(n):
            if n < 0 or n % 2 == 0 or n not in ev_tt or n in ev_shdma:
                return
            h = (n // 8) * 2 + 1
            cc = (n // 2) % 4
            ao_col = ao_off(h // 2) // 2 + 512 * cc
            sl = (n // 2) % 2
            ev_shdma[n] = sc.op("g", lambda g, ao_col=ao_col, sl=sl: g.dma_start(
                AR[64:128, ao_col:ao_col + 512], bslc(O_TMP + 1024 * sl, 0, 64, 512)),
                waits=[ev_tt[n]], inc=("g", 16))

        for b in range(nblocks):
            p, c = b // 4, b % 4
            nA, nB = 2 * b, 2 * b + 1
            j = p
            qo = (O_Q + 4096 * j) // 2
            ko = (O_K + 4096 * j) // 2
            nkt = 4 * c + 4
            ev_exp_this = {0: {}, 1: {}}
            for kt in range(nkt):
                pb = kt % 2
                is_diag = (kt >= 4 * c)
                for par, hb, Pbank in ((0, 0, PB), (1, 64, PA)):
                    Pst = Pbank[:, 1024 + 512 * pb: 1536 + 512 * pb]
                    n = nA + par
                    w = []
                    if c == 0 and kt < 2:
                        w += [ev_rope[p], ev_rope[8 + p]]
                    if b == 0 and kt < 2:
                        w += [ev_qsev_ch[62], ev_qsev_ch[63], ev_vevac[15]]
                    if ev_exp_bank.get((par, pb)) is not None:
                        w.append(ev_exp_bank[(par, pb)])
                    ev_s = sc.op("t", lambda t, Pst=Pst, kt=kt, c=c, qo=qo, ko=ko,
                                 hb=hb: t.matmul(
                        Pst,
                        AR[hb:hb + 64, ko + 128 * kt: ko + 128 * kt + 128],
                        AR[hb:hb + 64, qo + 512 * c: qo + 512 * c + 512],
                        start=True, stop=True),
                        waits=w, inc=("t", 1))
                    wv = [ev_s]
                    if (n - 2) in ev_expavWAR:
                        wv.append(ev_expavWAR[n - 2])
                    if b == 0 and par == 1 and kt == 0:
                        wv.append(ev_gate)
                    eoff = esb_off(par) + 1024 * kt
                    ev_e = sc.op("c", lambda c_, Pst=Pst, eoff=eoff:
                        c_.activation(bslc(eoff, 0, 128, 512), Pst, AF.Exp),
                        waits=wv, inc=("c", 1))
                    ev_exp_bank[(par, pb)] = ev_e
                    ev_exp_this[par][kt] = ev_e
            emit_rep(nA - 2)
            emit_rep(nB - 2)
            emit_shdma(nB - 4)
            ev_mask = {}
            for par in (0, 1):
                w = [ev_exp_this[par][kt] for kt in range(4 * c, 4 * c + 4)]
                if b == 0:
                    w.append(ev_m4)
                ev_mask[par] = sc.op("v", lambda v, par=par, c=c: v.tensor_tensor(
                    bslc(esb_off(par) + 4096 * c, 0, 128, 2048),
                    bslc(esb_off(par) + 4096 * c, 0, 128, 2048),
                    bslc(O_M4, 0, 128, 2048), ALU.mult),
                    waits=w, inc=("v", 1))
            emit_invb(nA - 2)
            emit_invb(nB - 2)
            emit_tt(nA - 2)
            emit_tt(nB - 2)
            for par in (0, 1):
                n = nA + par
                h = 2 * p + par
                Pav = PB[:, 512 * par:512 * par + 512]
                ev_a = None
                for kt in range(nkt):
                    w = [ev_mask[par]] if kt >= 4 * c else [ev_exp_this[par][kt]]
                    if kt == 0:
                        if (n - 2) in ev_tt:
                            w.append(ev_tt[n - 2])
                        if b == 0:
                            w.append(ev_vaug_all)
                    ev_a = sc.op("t", lambda t, Pav=Pav, kt=kt, h=h, par=par: t.matmul(
                        Pav[0:65, :],
                        AR[0:128, (O_VAUG + 2080 * kt) // 2 + 65 * h:
                                  (O_VAUG + 2080 * kt) // 2 + 65 * h + 65],
                        bslc(esb_off(par) + 1024 * kt, 0, 128, 512),
                        start=(kt == 0), stop=(kt == nkt - 1)),
                        waits=w, inc=("t", 1) if kt == nkt - 1 else None)
                ev_av[n] = ev_a
                ev_expavWAR[n] = ev_a
            for par in (0, 1):
                n = nA + par
                wv = [ev_av[n]]
                if (n - 2) in ev_repmm:
                    wv.append(ev_repmm[n - 2])

                def _recip(v, par=par):
                    with nc.allow_low_precision(reason="bf16 matmul operand"):
                        return v.reciprocal(
                            bslc(O_INVRZ + 1024 * par, 64, 1, 512),
                            PB[64:65, 512 * par:512 * par + 512])
                ev_inv[n] = sc.op("v", _recip, waits=wv, inc=("v", 1))
        for n in range(2 * nblocks - 2, 2 * nblocks + 6):
            emit_rep(n - 2)
            emit_invb(n - 2)
            emit_tt(n - 2)
            emit_shdma(n - 3)
        ev_heads_t = sc.ev("t")
        ev_heads_v = sc.ev("v")
        ev_heads_c = sc.ev("c")
        ev_heads_g = sc.ev("g")

        if n_phases <= 3:
            return _finish(nc, sc, block, taps, tap_dram, AR, ARf,
                           [ev_heads_t, ev_heads_v, ev_heads_c])

        ev_opevac = [None] * NTT
        for tt in range(NTT):
            par = tt % 2
            ev_x2 = sc.op("s", lambda s, tt=tt, par=par: s.dma_start(
                fslc(O_X2 + 4096 * par, 0, 128, DM),
                x_in[128 * tt:128 * tt + 128, :]),
                waits=[EV_TPOSE, ev_heads_t]
                      + ([ev_opevac[tt - 2]] if tt >= 2 else []),
                inc=("s", 16))
            Pop = (PA if par == 0 else PB)[:, 0:1024]
            w0 = [ev_heads_v, ev_heads_c, ev_heads_g]
            if tt >= 2:
                w0.append(ev_opevac[tt - 2])
            ev = None
            for m in range(NM):
                for cc in range(2):
                    ev = sc.op("t", lambda t, Pop=Pop, m=m, cc=cc, tt=tt: t.matmul(
                        Pop[:, 512 * cc:512 * cc + 512],
                        bslc(ao_off(m) + 2 * 128 * tt, 0, 128, 128),
                        bap3(O_WOT + 2 * 128 * (32 * cc + m), 1024, 4, 128),
                        start=(m == 0), stop=(m == NM - 1)),
                        waits=w0 if (m == 0 and cc == 0) else [],
                        inc=("t", 1) if (m == NM - 1 and cc == 1) else None)
            ev_opevac[tt] = sc.op("v", lambda v, Pop=Pop, tt=tt, par=par:
                v.tensor_tensor(
                    fslc(oa_off(tt), 0, 128, DM), Pop,
                    fslc(O_X2 + 4096 * par, 0, 128, DM), ALU.add),
                waits=[ev, ev_x2], inc=("v", 1))
        ev_op_all_v = sc.ev("v")

        if n_phases <= 4:
            return _finish(nc, sc, block, taps, tap_dram, AR, ARf, [ev_op_all_v])

        ev_w1 = {}
        ev_w3 = {}
        ev_w2last = [None] * NE
        ev_h1mm = {}
        ev_h3mm = {}
        ev_silu = {}
        ev_hh = {}
        ev_yevac = {}
        ev_yphase_end = [None] * NE

        def dma_w13(e, ht, waits1, waits3):
            par = ht % 2
            ev_w1[(e, ht)] = sc.op("g", lambda g, e=e, ht=ht, par=par: g.dma_start(
                bap(O_W1B + 2048 * par, [[128, 8], [1, 128]]),
                bass.AP(w1, DM * HFF * e + 128 * ht,
                        [[HFF, 128], [128 * HFF, 8], [1, 128]])),
                waits=waits1, inc=("g", 16))
            ev_w3[(e, ht)] = sc.op("g", lambda g, e=e, ht=ht, par=par: g.dma_start(
                bap(O_W3B + 2048 * par, [[128, 8], [1, 128]]),
                bass.AP(w3, DM * HFF * e + 128 * ht,
                        [[HFF, 128], [128 * HFF, 8], [1, 128]])),
                waits=waits3, inc=("g", 16))

        def dma_w2(e, waits):
            evw2 = None
            for ht in range(16):
                evw2 = sc.op("g", lambda g, e=e, ht=ht: g.dma_start(
                    bslc(w2_off(ht), 0, 128, 1024),
                    w2[e, 128 * ht:128 * ht + 128, :]),
                    waits=waits if ht == 0 else [], inc=("g", 16))
            ev_w2last[e] = evw2

        dma_w13(0, 0, [EV_TPOSE, ev_op_all_v], [])
        dma_w13(0, 1, [], [])
        dma_w2(0, [ev_op_all_v])

        for e in range(NE):
            for ht in range(16):
                par = ht % 2
                w1w = [ev_w1[(e, ht)]]
                if (e, ht - 1) in ev_silu:
                    w1w.append(ev_silu[(e, ht - 1)])
                elif e > 0:
                    w1w.append(ev_silu[(e - 1, 15)])
                if ht == 0:
                    if e == 0:
                        w1w.append(ev_op_all_v)
                    else:
                        w1w.append(ev_yevac[(e - 1, 14)])
                for m in range(NM):
                    for cc in range(4):
                        sc.op("t", lambda t, m=m, cc=cc, par=par: t.matmul(
                            PA[:, 512 * cc:512 * cc + 512],
                            bslc(O_W1B + 2048 * par + 2 * 128 * m, 0, 128, 128),
                            bap3(O_XNT + 2 * 128 * (32 * cc + m), 1024, 4, 128),
                            start=(m == 0), stop=(m == NM - 1)),
                            waits=w1w if (m == 0 and cc == 0) else [],
                            inc=("t", 1) if (m == NM - 1 and cc == 3) else None)
                ev_h1mm[(e, ht)] = sc.ev("t")
                w3w = [ev_w3[(e, ht)]]
                if (e, ht - 1) in ev_hh:
                    w3w.append(ev_hh[(e, ht - 1)])
                elif e > 0:
                    w3w.append(ev_hh[(e - 1, 15)])
                if ht == 0 and e > 0:
                    w3w.append(ev_yevac[(e - 1, 15)])
                for m in range(NM):
                    for cc in range(4):
                        sc.op("t", lambda t, m=m, cc=cc, par=par: t.matmul(
                            PB[:, 512 * cc:512 * cc + 512],
                            bslc(O_W3B + 2048 * par + 2 * 128 * m, 0, 128, 128),
                            bap3(O_XNT + 2 * 128 * (32 * cc + m), 1024, 4, 128),
                            start=(m == 0), stop=(m == NM - 1)),
                            waits=w3w if (m == 0 and cc == 0) else [],
                            inc=("t", 1) if (m == NM - 1 and cc == 3) else None)
                ev_h3mm[(e, ht)] = sc.ev("t")
                w = [ev_h1mm[(e, ht)]]
                if (e, ht - 1) in ev_hh:
                    w.append(ev_hh[(e, ht - 1)])
                elif e > 0:
                    w.append(ev_hh[(e - 1, 15)])
                sc.op("c", lambda c_: c_.activation(
                    bslc(O_S1, 0, 128, 1024), PA[:, 0:1024], AF.Silu),
                    waits=w)
                ev_silu[(e, ht)] = sc.op("c", lambda c_: c_.activation(
                    bslc(O_S1 + 2048, 0, 128, 1024), PA[:, 1024:2048], AF.Silu),
                    inc=("c", 1))
                ev_hh[(e, ht)] = sc.op("v", lambda v, ht=ht: v.scalar_tensor_tensor(
                    bslc(hh_off(ht), 0, 128, 2048), PB[:, :], 1.0,
                    bslc(O_S1, 0, 128, 2048), ALU.mult, ALU.mult),
                    waits=[ev_h3mm[(e, ht)], ev_silu[(e, ht)]], inc=("v", 1))
                if ht + 2 < 16:
                    dma_w13(e, ht + 2, [ev_h1mm[(e, ht)]], [ev_h3mm[(e, ht)]])
                elif ht == 15 and e + 1 < NE:
                    dma_w13(e + 1, 0, [ev_h1mm[(e, 14)]], [ev_h3mm[(e, 14)]])
                    dma_w13(e + 1, 1, [ev_h1mm[(e, 15)]], [ev_h3mm[(e, 15)]])
            for tt in range(NTT):
                par = tt % 2
                Py = (PA if par == 0 else PB)[:, 0:1024]
                w0 = [ev_hh[(e, 15)], ev_w2last[e]]
                if tt >= 2:
                    w0.append(ev_yevac[(e, tt - 2)])
                ev = None
                for ht in range(16):
                    for cc in range(2):
                        ev = sc.op("t", lambda t, Py=Py, ht=ht, cc=cc, tt=tt: t.matmul(
                            Py[:, 512 * cc:512 * cc + 512],
                            bslc(hh_off(ht) + 2 * 128 * tt, 0, 128, 128),
                            bslc(w2_off(ht) + 2 * 512 * cc, 0, 128, 512),
                            start=(ht == 0), stop=(ht == 15)),
                            waits=w0 if (ht == 0 and cc == 0) else [],
                            inc=("t", 1) if (ht == 15 and cc == 1) else None)
                sc.op("v", lambda v, Py=Py, tt=tt, e=e: v.scalar_tensor_tensor(
                    fslc(oa_off(tt), 0, 128, 512), Py[:, 0:512],
                    fslc(O_GS + 4 * (8 * tt + e), 0, 128, 1),
                    fslc(oa_off(tt), 0, 128, 512), ALU.mult, ALU.add),
                    waits=[ev, ev_gate])
                ev_yevac[(e, tt)] = sc.op("v", lambda v, Py=Py, tt=tt, e=e:
                    v.scalar_tensor_tensor(
                        fslc(oa_off(tt) + 2048, 0, 128, 512), Py[:, 512:1024],
                        fslc(O_GS + 4 * (8 * tt + e), 0, 128, 1),
                        fslc(oa_off(tt) + 2048, 0, 128, 512),
                        ALU.mult, ALU.add),
                    inc=("v", 1))
            ev_yphase_end[e] = ev_yevac[(e, NTT - 1)]
            if e + 1 < NE:
                dma_w2(e + 1, [ev_yphase_end[e]])

        for tt in range(NTT):
            sc.op("s", lambda s, tt=tt: s.dma_start(
                out[128 * tt:128 * tt + 128, :], fslc(oa_off(tt), 0, 128, DM)),
                waits=[ev_yevac[(NE - 1, tt)]], inc=("s", 16))
        return _finish(nc, sc, block, taps, tap_dram, AR, ARf, [])


def _finish(nc, sc, block, taps, tap_dram, AR, ARf, final_evs):
    for (name, rows, cols, dt, off) in taps:
        src = AR if dt == BF16 else ARf
        div = 2 if dt == BF16 else 4
        sc.op("s", lambda s, name=name, off=off, rows=rows, cols=cols, src=src,
              div=div: s.dma_start(
                  tap_dram[name].ap(), src[0:rows, off // div: off // div + cols]),
              waits=[sc.ev(k) for k in ("t", "v", "c")] + list(final_evs),
              inc=("s", 16))
    sc.op("s", lambda s: None, waits=[("s", sc.cnt["s"])])

    @block.tensor
    def _(t):
        sc.emit("t", t)

    @block.vector
    def _(v):
        sc.emit("v", v)

    @block.scalar
    def _(c):
        sc.emit("c", c)

    @block.gpsimd
    def _(g):
        sc.emit("g", g)

    @block.sync
    def _(s):
        sc.emit("s", s)
    return nc


def _host_consts():
    inv_freq = (10000.0 ** (-np.arange(0, DK, 2, dtype=np.float32) / DK))
    t = np.arange(S, dtype=np.float32)
    ang = np.outer(inv_freq, t)
    cosT = np.cos(ang).astype(np.float32)
    sinT = np.sin(ang).astype(np.float32)
    C128 = np.zeros((128, S), np.float32)
    S128 = np.zeros((128, S), np.float32)
    for p in range(128):
        i = (p % 64) // 2
        C128[p] = cosT[i]
        S128[p] = sinT[i] * (-1.0 if p % 2 == 0 else 1.0)
    PSW = np.zeros((128, 128), np.float32)
    for p in range(128):
        PSW[p ^ 1, p] = 1.0
    PSH = np.zeros((128, 128), np.float32)
    for kk in range(64):
        PSH[kk, kk + 64] = 1.0
    M = np.ones((128, 2048), dtype=np.float32)
    for jp in range(4):
        blk = M[:, 512 * jp:512 * jp + 512]
        blk[:, :128 * jp] = 0.0
        for kk in range(128):
            blk[kk, 128 * jp:128 * jp + kk] = 0.0
    I = np.eye(128, dtype=np.float32)
    cast = lambda a: np.ascontiguousarray(a.astype(bfdt))
    return dict(ctab=cast(C128), stab=cast(S128), psw=cast(PSW), psh=cast(PSH),
                mlog=cast(M), ident=cast(I))


def kernel(L0_attn_x, L0_attn_W_qkv, L0_attn_W_o, L0_ffn_w_gate,
           L0_ffn_w1, L0_ffn_w3, L0_ffn_w2, _n_phases=99, _taps=(), _nc=None,
           _nblocks=32):
    X = np.ascontiguousarray(np.asarray(L0_attn_x, dtype=np.float32))
    base = dict(wqkv=np.ascontiguousarray(np.asarray(L0_attn_W_qkv, np.float32)),
                wo=np.ascontiguousarray(np.asarray(L0_attn_W_o, np.float32)),
                wg=np.ascontiguousarray(np.asarray(L0_ffn_w_gate, np.float32)),
                w1=np.ascontiguousarray(np.asarray(L0_ffn_w1, np.float32)),
                w3=np.ascontiguousarray(np.asarray(L0_ffn_w3, np.float32)),
                w2=np.ascontiguousarray(np.asarray(L0_ffn_w2, np.float32)),
                **_host_consts())
    nc = _nc if _nc is not None else build(n_phases=_n_phases, taps=_taps, nblocks=_nblocks)
    in_maps = [dict(base, x=np.ascontiguousarray(X[b])) for b in range(8)]
    res = run_bass_kernel_spmd(nc, in_maps, core_ids=list(range(8)))
    if _taps:
        return res
    return np.stack([np.asarray(res.results[b]["out"]) for b in range(8)], axis=0)
